# version 01
# baseline (speedup 1.0000x reference)
from contextlib import ExitStack

import numpy as np
import concourse.bass as bass
import concourse.mybir as mybir
from concourse.bass_utils import run_bass_kernel_spmd

F32 = mybir.dt.float32
BF16 = mybir.dt.bfloat16
AF = mybir.ActivationFunctionType
ALU = mybir.AluOpType
AX = mybir.AxisListType.X

NCORE = 8
R = 8
TOK = 2048
SEQ = 4096
NPAIR = 8
NKC = 32
NE = 8


def _lhsT_slab(M):
    K, N = M.shape
    r = K // 128
    return np.ascontiguousarray(M.reshape(r, 128, N).transpose(1, 0, 2).reshape(128, r * N))


def _featmaj(xh):
    T = xh.shape[0]
    return np.ascontiguousarray(xh.T.reshape(R, 128, T).transpose(1, 0, 2).reshape(128, R * T))


def _unfeatmaj(xo):
    T = xo.shape[1] // R
    return xo.reshape(128, R, T).transpose(2, 1, 0).reshape(T, R * 128)


def _bias_col(b):
    return np.ascontiguousarray(b.reshape(R, 128).T)


def _gauss_windows(q0, nq):
    out = []
    for kc in range(NKC):
        a = max(128 * (kc - 1), q0)
        bb = min(128 * (kc + 2), q0 + nq)
        if a < bb:
            out.append((kc, a - q0, bb - q0, a - 128 * (kc - 1), bb - 128 * (kc - 1)))
    return out


class Sem:
    def __init__(self, es, nc, name):
        self.h = es.enter_context(nc.semaphore(name))
        self.n = 0


class B:
    def __init__(self, nc, es):
        self.nc = nc
        self.ops = {e: [] for e in ("pe", "act", "dve", "gp", "sy")}
        self.s = {e: Sem(es, nc, "s_" + e)
                  for e in ("pe", "act", "dve", "gp", "sy", "gpc")}

    def op(self, eng, fn, waits=(), inc=True, inck=1, sem=None):
        w = tuple((self.s[sn].h, v) for sn, v in waits if v > 0)
        s = self.s[sem if sem else eng]

        def run(h, w=w, fn=fn, inc=inc, inck=inck, sh=s.h):
            for hh, v in w:
                h.wait_ge(hh, v)
            ins = fn(h)
            if inc:
                ins.then_inc(sh, inck)
        self.ops[eng].append(run)
        if inc:
            s.n += inck
        return s.n

    def guard(self):
        return {e: self.s[e].n for e in self.s}

    def nsem(self, name):
        return self.s[name].n

    def gw(self, g, skip=()):
        return [(e, v) for e, v in g.items() if e not in skip]

    def n(self, eng):
        return self.s[eng].n


def build_layer(li, dbg=False):
    nc = bass.Bass("TRN2", target_bir_lowering=False)
    dt = nc.dram_tensor
    xt_a = dt("xt_a", [128, R * TOK], F32, kind="ExternalInput").ap()
    xt_b = dt("xt_b", [128, R * TOK], F32, kind="ExternalInput").ap()
    xt_o = dt("xt_o", [128, R * TOK], F32, kind="ExternalInput").ap()
    xo_d = dt("xo", [128, R * TOK], F32, kind="ExternalOutput").ap()
    wq_d = dt("wq", [128, R * 1024], F32, kind="ExternalInput").ap()
    wk_d = dt("wk", [128, R * 1024], F32, kind="ExternalInput").ap()
    wv_d = dt("wv", [128, R * 1024], F32, kind="ExternalInput").ap()
    wo_d = dt("wo", [128, R * 1024], F32, kind="ExternalInput").ap()
    bq_d = dt("bq", [128, 8], F32, kind="ExternalInput").ap()
    bk_d = dt("bk", [128, 8], F32, kind="ExternalInput").ap()
    bv_d = dt("bv", [128, 8], F32, kind="ExternalInput").ap()
    sel_d = dt("sel", [1, 4], F32, kind="ExternalInput").ap()
    if li == 0:
        bo_d = dt("bo", [128, 8], F32, kind="ExternalInput").ap()
        lam_d = dt("lam", [1, 1], F32, kind="ExternalInput").ap()
        wf_d = dt("wf", [128, R * 1024], F32, kind="ExternalInput").ap()
    else:
        ema_d = dt("emask_a", [64, 1], F32, kind="ExternalInput").ap()
        emb_d = dt("emask_b", [1, 1], F32, kind="ExternalInput").ap()
        rw1_d = dt("rw1", [128, R * 4096], F32, kind="ExternalInput").ap()
        rb1_d = dt("rb1", [128, 32], F32, kind="ExternalInput").ap()
        rw2_d = dt("rw2", [128, 32 * 8], F32, kind="ExternalInput").ap()
        rb2_d = dt("rb2", [8, 1], F32, kind="ExternalInput").ap()
        w1_d = dt("w1", [NE, 2, 128, R * 2048], F32, kind="ExternalInput").ap()
        b1_d = dt("b1", [NE, 128, 32], F32, kind="ExternalInput").ap()
        w2_d = dt("w2", [NE, 2, 128, 16 * 1024], F32, kind="ExternalInput").ap()
        b2_d = dt("b2", [8, 1024], F32, kind="ExternalInput").ap()
        xn_o_d = dt("xn_o_scr", [128, R * TOK], BF16, kind="Internal").ap()
        if dbg:
            dbg_ao = dt("dbg_ao", [128, R * TOK], BF16, kind="ExternalOutput").ap()
            dbg_lt = dt("dbg_lt", [8, 2048], F32, kind="ExternalOutput").ap()
            dbg_we = dt("dbg_we", [8, 2048], F32, kind="ExternalOutput").ap()
            dbg_x3 = dt("dbg_x3", [128, R * TOK], F32, kind="ExternalOutput").ap()
            dbg_xn = dt("dbg_xn", [128, R * TOK], BF16, kind="ExternalOutput").ap()

    es = ExitStack()
    with es:
        sbuf = lambda name, shape, dtp: es.enter_context(nc.sbuf_tensor(name, shape, dtp))
        b = B(nc, es)
        E = b.op

        x_sb = sbuf("x_sb", [128, R * TOK], F32)
        ones_bf = sbuf("ones_bf", [128, 128], BF16)
        ones_f = sbuf("ones_f", [128, 128], F32)
        idn_bf = sbuf("idn_bf", [128, 128], BF16)
        idn_f = sbuf("idn_f", [128, 128], F32)
        selc = sbuf("selc", [128, 4], F32)
        bqc = sbuf("bqc", [128, 8], F32)
        bkc = sbuf("bkc", [128, 8], F32)
        bvc = sbuf("bvc", [128, 8], F32)
        lam12 = sbuf("lam12", [128, 1], F32)
        cbias = sbuf("cbias", [128, 4], F32)
        if li == 0:
            boc = sbuf("boc", [128, 8], F32)
            lamc = sbuf("lamc", [128, 1], F32)
            b3a = sbuf("b3a", [128, 384], F32)
            b3b = sbuf("b3b", [128, 384], F32)
        else:
            ema = sbuf("ema", [128, 1], F32)
            emb = sbuf("emb", [1, 1], F32)
        iota_es = ExitStack()
        iota_t = iota_es.enter_context(nc.sbuf_tensor("iota_t", [128, 384], F32))

        d_b3 = 0
        with nc.psum_tensor("ps_m", [128, 8], F32) as ps_m:
            E("sy", lambda s: s.dma_start(selc[0:1, 0:4], sel_d), inck=16)
            E("sy", lambda s: s.dma_start(bqc[:], bq_d), inck=16)
            E("sy", lambda s: s.dma_start(bkc[:], bk_d), inck=16)
            E("sy", lambda s: s.dma_start(bvc[:], bv_d), inck=16)
            if li == 0:
                E("sy", lambda s: s.dma_start(boc[:], bo_d), inck=16)
                E("sy", lambda s: s.dma_start(lamc[0:1, 0:1], lam_d), inck=16)
            else:
                E("sy", lambda s: s.dma_start(ema[64:128, :], ema_d), inck=16)
                E("sy", lambda s: s.dma_start(emb[:], emb_d), inck=16)
            n_sy0 = b.n("sy")
            g_x = E("gp", lambda g: g.dma_start(x_sb[:], xt_o), inck=16)

            def c_misc(v):
                v.memset(ones_bf[:], 1.0)
                v.memset(ones_f[:], 1.0)
                v.memset(cbias[:, 0:1], -1.0)
                v.memset(cbias[:, 1:2], -30.0)
                v.memset(cbias[:, 2:3], 1e-5)
                if li == 1:
                    v.memset(lam12[:], 0.0)
                return v.nop()
            d_ones = E("dve", c_misc)

            g_iota = E("gp", lambda g: g.iota(
                iota_t[:], pattern=[[1, 384]], base=-128, channel_multiplier=-1,
                allow_small_or_imprecise_dtypes=True), sem="gpc")

            def c_idn(v):
                v.tensor_scalar(idn_f[:], iota_t[:, 0:128], -128.0, None, ALU.is_equal)
                v.drain()
                return v.tensor_copy(idn_bf[:], idn_f[:])
            d_idn = E("dve", c_idn, waits=[("gpc", g_iota)])

            def c_rep(t):
                ins = t.matmul(ps_m[:, 0:4], ones_f[0:1, :], selc[0:1, 0:4], start=True,
                               stop=True, skip_group_check=True)
                if li == 0:
                    ins = t.matmul(ps_m[:, 4:5], ones_f[0:1, :], lamc[0:1, 0:1], start=True,
                                   stop=True, skip_group_check=True)
                return ins
            p_rep = E("pe", c_rep, waits=[("dve", d_idn), ("sy", n_sy0)])

            def c_repv(v):
                v.tensor_copy(selc[:, 0:4], ps_m[:, 0:4])
                if li == 0:
                    v.tensor_copy(lamc[:], ps_m[:, 4:5])
                    v.drain()
                    v.tensor_scalar(cbias[:, 3:4], lamc[:], -1.0, None, ALU.mult)
                    v.drain()
                    v.tensor_tensor(lam12[:], lamc[:], cbias[:, 3:4], ALU.max)
                    v.drain()
                    v.tensor_scalar(lam12[:], lam12[:], 1.2, None, ALU.mult)
                    v.drain()
                return v.tensor_scalar(bqc[:], bqc[:], 0.125, None, ALU.mult)
            d_rep = E("dve", c_repv, waits=[("pe", p_rep)])

            if li == 0:
                with nc.sbuf_tensor("b3t", [128, 768], F32) as b3t:
                    def c_b3(a):
                        a.activation(b3t[:, 0:384], iota_t[:], AF.Abs)
                        a.drain()
                        a.activation(b3t[:, 384:768], b3t[:, 0:384], AF.Square,
                                     bias=cbias[:, 0:1])
                        a.drain()
                        a.activation(b3a[:], b3t[:, 384:768], AF.Exp, scale=-2.0)
                        a.drain()
                        a.activation(b3t[:, 384:768], b3t[:, 0:384], AF.Square,
                                     bias=cbias[:, 1:2])
                        a.drain()
                        return a.activation(b3b[:], b3t[:, 384:768], AF.Exp, scale=-0.125)
                    a_b3 = E("act", c_b3, waits=[("gpc", g_iota), ("dve", d_ones)])

                    def c_b3c(v):
                        v.tensor_scalar(b3a[:], b3a[:], 0.8, None, ALU.mult)
                        v.tensor_scalar(b3b[:], b3b[:], 0.4, None, ALU.mult)
                        v.drain()
                        v.tensor_tensor(b3t[:, 0:384], b3a[:], b3b[:], ALU.add)
                        v.drain()
                        v.tensor_scalar(b3a[:], b3t[:, 0:384], lamc[:, 0:1], selc[:, 2:3],
                                        ALU.mult, ALU.mult)
                        return v.tensor_scalar(b3b[:], b3t[:, 0:384], lamc[:, 0:1],
                                               selc[:, 3:4], ALU.mult, ALU.mult)
                    d_b3 = E("dve", c_b3c, waits=[("act", a_b3), ("dve", d_rep)])

        iota_es.close()

        with nc.sbuf_tensor("xn_a", [128, R * TOK], BF16) as xn_a, \
             nc.sbuf_tensor("xn_b", [128, R * TOK], BF16) as xn_b:

            with nc.sbuf_tensor("xh", [128, 4 * TOK], F32) as xh, \
                 nc.sbuf_tensor("scrb", [128, TOK], BF16) as scrb, \
                 nc.sbuf_tensor("scrq", [128, TOK], BF16) as scrq, \
                 nc.sbuf_tensor("st_m", [128, TOK], F32) as st_m, \
                 nc.sbuf_tensor("st_r", [128, TOK], F32) as st_r, \
                 nc.sbuf_tensor("st_t", [128, TOK], F32) as st_t, \
                 nc.psum_tensor("ps_s1", [128, 2048], F32) as ps_s1, \
                 nc.psum_tensor("ps_s2", [128, 2048], F32) as ps_s2:

                def ln_half(src_dram, dst):
                    p_prev = [0]
                    for hslab in range(2):
                        gg = b.guard()
                        n_ld = E("sy", lambda s, hs=hslab, src=src_dram: s.dma_start(
                            xh[:], src.rearrange("p (r t) -> p r t", r=R)[:, hs * 4:(hs + 1) * 4, :]),
                            waits=b.gw(gg, skip=("sy",)), inck=16)
                        for rr in range(4):
                            r = hslab * 4 + rr
                            d_c = E("dve", lambda v, rr=rr: v.tensor_copy(
                                scrb[:], xh[:, rr * TOK:(rr + 1) * TOK]),
                                waits=[("sy", n_ld), ("pe", p_prev[0])])
                            a_c = E("act", lambda a, rr=rr: a.activation(
                                scrq[:], xh[:, rr * TOK:(rr + 1) * TOK], AF.Square),
                                waits=[("sy", n_ld), ("pe", p_prev[0])])

                            def mm(t, r=r):
                                for g in range(4):
                                    t.matmul(ps_s1[:, g * 512:(g + 1) * 512], ones_bf[:],
                                             scrb[:, g * 512:(g + 1) * 512],
                                             start=(r == 0), stop=(r == R - 1),
                                             skip_group_check=True)
                                for g in range(4):
                                    i2 = t.matmul(ps_s2[:, g * 512:(g + 1) * 512], ones_bf[:],
                                                  scrq[:, g * 512:(g + 1) * 512],
                                                  start=(r == 0), stop=(r == R - 1),
                                                  skip_group_check=True)
                                return i2
                            p_prev[0] = E("pe", mm, waits=[("dve", d_c), ("act", a_c)])

                    def stat1(v):
                        v.tensor_scalar(st_m[:], ps_s1[:], 1.0 / 1024.0, None, ALU.mult)
                        return v.tensor_scalar(st_r[:], ps_s2[:], 1.0 / 1024.0, None, ALU.mult)
                    d_s1 = E("dve", stat1, waits=[("pe", p_prev[0])])
                    a_s2 = E("act", lambda a: a.activation(st_t[:], st_m[:], AF.Square),
                             waits=[("dve", d_s1)])
                    d_s3 = E("dve", lambda v: v.tensor_tensor(st_r[:], st_r[:], st_t[:],
                                                              ALU.subtract), waits=[("act", a_s2)])

                    def rstd(a):
                        a.activation(st_r[:], st_r[:], AF.Ln, bias=cbias[:, 2:3])
                        return a.activation(st_r[:], st_r[:], AF.Exp, scale=-0.5)
                    a_s4 = E("act", rstd, waits=[("dve", d_s3)])
                    d_s5 = E("dve", lambda v: v.tensor_tensor(st_t[:], st_m[:], st_r[:], ALU.mult),
                             waits=[("act", a_s4)])

                    for hslab in range(2):
                        n_ld2 = E("sy", lambda s, hs=hslab, src=src_dram: s.dma_start(
                            xh[:], src.rearrange("p (r t) -> p r t", r=R)[:, hs * 4:(hs + 1) * 4, :]),
                            waits=[("dve", d_s5 if hslab == 0 else b.n("dve"))], inck=16)
                        for rr in range(4):
                            r = hslab * 4 + rr

                            def xn1(v, rr=rr, r=r, dst=dst):
                                v.tensor_tensor(st_m[:], xh[:, rr * TOK:(rr + 1) * TOK],
                                                st_r[:], ALU.mult)
                                return v.tensor_tensor(dst[:, r * TOK:(r + 1) * TOK], st_m[:],
                                                       st_t[:], ALU.subtract)
                            E("dve", xn1, waits=[("sy", n_ld2)])

                ln_half(xt_a, xn_a)
                ln_half(xt_b, xn_b)

                if li == 1:
                    for r in range(R):
                        def mix(v, r=r):
                            v.tensor_scalar(scrb[:], xn_a[:, r * TOK:(r + 1) * TOK],
                                            selc[:, 0:1], None, ALU.mult)
                            v.tensor_scalar(scrq[:], xn_b[:, r * TOK:(r + 1) * TOK],
                                            selc[:, 1:2], None, ALU.mult)
                            return v.tensor_tensor(scrb[:], scrb[:], scrq[:], ALU.add)
                        d_mx = E("dve", mix, waits=[("dve", d_rep), ("sy", b.n("sy"))])
                        E("sy", lambda s, r=r: s.dma_start(
                            xn_o_d.rearrange("p (r t) -> p r t", r=R)[:, r, :], scrb[:]),
                            waits=[("dve", d_mx)], inck=16)

            with nc.sbuf_tensor("aoT", [128, NPAIR * TOK], BF16) as aoT:
                with nc.sbuf_tensor("qt", [128, TOK], BF16) as qt, \
                     nc.sbuf_tensor("kt", [128, SEQ], BF16) as kt, \
                     nc.sbuf_tensor("v_sb", [128, NKC * 130], BF16) as v_sb, \
                     nc.sbuf_tensor("exp0", [128, 2048], BF16) as exp0, \
                     nc.sbuf_tensor("exp1", [128, 2048], BF16) as exp1, \
                     nc.sbuf_tensor("w_sl", [128, 3 * 1024], BF16) as w_sl, \
                     nc.sbuf_tensor("wab", [128, 2 * 1024], BF16) as wab, \
                     nc.sbuf_tensor("rcp", [128, 512], F32) as rcp, \
                     nc.sbuf_tensor("lrep", [64, 512], F32) as lrep, \
                     nc.sbuf_tensor("cmx", [128, 14], F32) as cmx, \
                     nc.sbuf_tensor("negc", [128, 1], F32) as negc:

                    exp_sb = (exp0, exp1)

                    def v_evac_transpose(pr, half, bc, p_p, d_tp, ps_qk, ps_tr):
                        a_prev = E("dve", lambda v, pr=pr, bc=bc, p_p=p_p:
                                   v.tensor_scalar(exp0[:], ps_qk[:], bc[:, pr:pr + 1],
                                                   None, ALU.add),
                                   waits=[("pe", p_p), ("dve", d_tp)])
                        for kcl in range(16):
                            kc = half * 16 + kcl
                            p_t = E("pe", lambda t, kcl=kcl: t.transpose(
                                ps_tr[:], exp0[:, kcl * 128:(kcl + 1) * 128], idn_bf[:]),
                                waits=[("dve", a_prev), ("dve", d_tp)])

                            def vt(v, kc=kc):
                                v.tensor_copy(v_sb[:, kc * 130:kc * 130 + 64], ps_tr[:, 0:64])
                                return v.tensor_copy(v_sb[:, kc * 130 + 65:kc * 130 + 129],
                                                     ps_tr[:, 64:128])
                            d_tp = E("dve", vt, waits=[("pe", p_t)])
                        return a_prev, d_tp

                    def attn_qg(pr, qg, d_negc, d_vdone):
                        with nc.psum_tensor(f"pav0_{pr}_{qg}", [65, 1024], F32) as pav0, \
                             nc.psum_tensor(f"pav1_{pr}_{qg}", [65, 1024], F32) as pav1:
                            exp_cnt = []
                            av_cnt = []
                            with nc.psum_tensor(f"psc_{pr}_{qg}", [128, 2048], F32) as psc:
                                for kc in range(NKC):
                                    w_s = ([("dve", d_negc), ("dve", d_vdone),
                                            ("dve", d_prev_norm[0])] if kc == 0 else
                                           [("act", exp_cnt[kc - 1])])

                                    def peS(t, kc=kc, qg=qg):
                                        for g in range(2):
                                            for h, tp, po in ((0, (0, 0), 0), (1, (64, 0), 64)):
                                                ins = t.matmul(
                                                    psc[:, h * 1024 + g * 512:h * 1024 + (g + 1) * 512],
                                                    kt[po:po + 64, kc * 128:(kc + 1) * 128],
                                                    qt[po:po + 64, qg * 1024 + g * 512:qg * 1024 + (g + 1) * 512],
                                                    start=True, stop=True,
                                                    skip_group_check=True, tile_position=tp)
                                        return ins
                                    p_s = E("pe", peS, waits=w_s)

                                    d_mask = 0
                                    if li == 0:
                                        adds = []
                                        for cls in range(2):
                                            q0 = cls * 2048 + qg * 1024
                                            for (kcw, la, lb, ua, ub) in _gauss_windows(q0, 1024):
                                                if kcw == kc:
                                                    adds.append((cls, la, lb, ua, ub))
                                        if adds:
                                            def dm(v, adds=tuple(adds)):
                                                for (cls, la, lb, ua, ub) in adds:
                                                    blk = b3a if cls == 0 else b3b
                                                    for h in range(2):
                                                        ins = v.tensor_tensor(
                                                            psc[:, h * 1024 + la:h * 1024 + lb],
                                                            psc[:, h * 1024 + la:h * 1024 + lb],
                                                            blk[:, ua:ub], ALU.add)
                                                return ins
                                            d_mask = E("dve", dm,
                                                       waits=[("pe", p_s), ("dve", d_b3)])

                                    w_e = [("pe", p_s)]
                                    if d_mask:
                                        w_e.append(("dve", d_mask))
                                    if kc >= 2:
                                        w_e.append(("pe", av_cnt[kc - 2]))
                                    a_e = E("act", lambda a, buf=kc % 2: a.activation(
                                        exp_sb[buf][:], psc[:], AF.Exp, bias=negc[:, 0:1]),
                                        waits=w_e)
                                    exp_cnt.append(a_e)

                                    d_em = 0
                                    if li == 1 and ((qg == 0 and kc == 31) or
                                                    (qg == 1 and kc == 0)):
                                        def em(v, qg=qg, buf=kc % 2):
                                            for h in range(2):
                                                if qg == 0:
                                                    ins = v.tensor_tensor(
                                                        exp_sb[buf][64:128, h * 1024:h * 1024 + 1],
                                                        exp_sb[buf][64:128, h * 1024:h * 1024 + 1],
                                                        ema[64:128, :], ALU.mult)
                                                else:
                                                    ins = v.tensor_scalar(
                                                        exp_sb[buf][0:1, h * 1024 + 1023:h * 1024 + 1024],
                                                        exp_sb[buf][0:1, h * 1024 + 1023:h * 1024 + 1024],
                                                        emb[0:1, 0:1], None, ALU.mult)
                                            return ins
                                        d_em = E("dve", em, waits=[("act", a_e)])

                                    w_av = [("act", a_e)] + ([("dve", d_em)] if d_em else [])

                                    def peA(t, kc=kc, buf=kc % 2):
                                        for h, pav in ((0, pav0), (1, pav1)):
                                            for g in range(2):
                                                ins = t.matmul(
                                                    pav[:, g * 512:(g + 1) * 512],
                                                    v_sb[:, kc * 130 + h * 65:kc * 130 + h * 65 + 65],
                                                    exp_sb[buf][:, h * 1024 + g * 512:h * 1024 + (g + 1) * 512],
                                                    start=(kc == 0), stop=(kc == NKC - 1),
                                                    skip_group_check=True)
                                        return ins
                                    p_a = E("pe", peA, waits=w_av)
                                    av_cnt.append(p_a)

                            with nc.psum_tensor(f"prp_{pr}_{qg}", [64, 512], F32) as prp, \
                                 nc.psum_tensor(f"psh_{pr}_{qg}", [128, 1024], F32) as psh:
                                aot1 = exp0[0:64, 0:1024]

                                d_prev = 0
                                for h, pav in ((0, pav0), (1, pav1)):
                                    orow = ones_f[0:1, 0:64] if h == 0 else ones_f[64:65, 0:64]
                                    rrow = rcp[0:1, :] if h == 0 else rcp[64:65, :]
                                    for g in range(2):
                                        def rcg(v, rrow=rrow, pav=pav, g=g):
                                            v.reciprocal(rrow,
                                                         pav[64:65, g * 512:(g + 1) * 512])
                                            return v.drain()
                                        d_prev = E("dve", rcg,
                                                   waits=[("pe", av_cnt[-1]),
                                                          ("dve", d_prev)])
                                        p_rp = E("pe", lambda t, orow=orow, rrow=rrow:
                                                 t.matmul(prp[:], orow, rrow,
                                                          start=True, stop=True,
                                                          skip_group_check=True),
                                                 waits=[("dve", d_prev)])
                                        def lrcp(v):
                                            v.tensor_copy(lrep[:], prp[:])
                                            return v.drain()
                                        d_lr = E("dve", lrcp, waits=[("pe", p_rp)])

                                        def nm(v, h=h, pav=pav, g=g, pr=pr, qg=qg):
                                            dst = (aoT[0:64, pr * 2048 + qg * 1024 + g * 512:
                                                       pr * 2048 + qg * 1024 + (g + 1) * 512]
                                                   if h == 0 else
                                                   aot1[:, g * 512:(g + 1) * 512])
                                            return v.tensor_tensor(
                                                dst, pav[0:64, g * 512:(g + 1) * 512],
                                                lrep[:], ALU.mult)
                                        d_prev = E("dve", nm, waits=[("dve", d_lr)])

                                def sh(t):
                                    for g in range(2):
                                        ins = t.matmul(psh[64:128, g * 512:(g + 1) * 512],
                                                       idn_bf[0:64, 0:64],
                                                       aot1[:, g * 512:(g + 1) * 512],
                                                       start=True, stop=True,
                                                       skip_group_check=True,
                                                       tile_position=(0, 64))
                                    return ins
                                p_sh = E("pe", sh, waits=[("dve", d_prev)])
                                d_sh = E("dve", lambda v, pr=pr, qg=qg: v.tensor_copy(
                                    aoT[64:128, pr * 2048 + qg * 1024:pr * 2048 + (qg + 1) * 1024],
                                    psh[64:128, :]), waits=[("pe", p_sh)])
                                d_prev_norm[0] = d_sh

                    d_prev_norm = [d_rep]
                    for pr in range(NPAIR):
                        gP = b.guard()

                        for i, wd in enumerate((wq_d, wk_d, wv_d)):
                            n_w = E("gp", lambda g, i=i, wd=wd, pr=pr: g.dma_start(
                                w_sl.rearrange("p (i r c) -> p i r c", i=3, r=R)[:, i],
                                wd.rearrange("p (r o) -> p r o", r=R)[
                                    :, :, pr * 128:(pr + 1) * 128]),
                                waits=(b.gw(gP, skip=("gp",)) if i == 0 else []), inck=16)

                        def qsc(v):
                            v.tensor_scalar(wab[:, 0:1024], w_sl[:, 0:1024], selc[:, 0:1],
                                            None, ALU.mult)
                            return v.tensor_scalar(wab[:, 1024:2048], w_sl[:, 0:1024],
                                                   selc[:, 1:2], None, ALU.mult)
                        d_ws = E("dve", qsc, waits=[("gp", n_w), ("dve", d_rep),
                                                    ("pe", gP["pe"])])

                        with nc.psum_tensor(f"ps_qk_{pr}", [128, 2048], F32) as ps_qk, \
                             nc.psum_tensor(f"ps_nrm_{pr}", [128, 512], F32) as ps_nrm, \
                             nc.psum_tensor(f"ps_tr_{pr}", [128, 128], BF16) as ps_tr:

                            def peQ(t):
                                for g in range(4):
                                    for wofs, src in ((0, xn_a), (1024, xn_b)):
                                        for r in range(R):
                                            ins = t.matmul(
                                                ps_qk[:, g * 512:(g + 1) * 512],
                                                wab[:, wofs + r * 128:wofs + (r + 1) * 128],
                                                src[:, r * TOK + g * 512:r * TOK + (g + 1) * 512],
                                                start=(wofs == 0 and r == 0),
                                                stop=(wofs == 1024 and r == R - 1),
                                                skip_group_check=True)
                                return ins
                            p_q = E("pe", peQ, waits=[("dve", d_ws)])
                            a_q = E("dve", lambda v, pr=pr: v.tensor_scalar(
                                qt[:], ps_qk[:], 0.125, bqc[:, pr:pr + 1], ALU.mult, ALU.add),
                                waits=[("pe", p_q)])

                            d_vm = E("dve", lambda v: v.memset(v_sb[:], 1.0),
                                     waits=[("pe", gP["pe"])])

                            a_prev = ("dve", a_q)
                            d_tp = d_vm
                            for idx, (wofs, bc) in enumerate(
                                    ((1024, bkc), (1024, bkc), (2048, bvc), (2048, bvc))):
                                src = (xn_a, xn_b)[idx % 2]
                                half = idx % 2

                                def peP(t, wofs=wofs, src=src):
                                    for g in range(4):
                                        for r in range(R):
                                            ins = t.matmul(
                                                ps_qk[:, g * 512:(g + 1) * 512],
                                                w_sl[:, wofs + r * 128:wofs + (r + 1) * 128],
                                                src[:, r * TOK + g * 512:r * TOK + (g + 1) * 512],
                                                start=(r == 0), stop=(r == R - 1),
                                                skip_group_check=True)
                                    return ins
                                p_p = E("pe", peP, waits=[(a_prev[0], a_prev[1])])
                                if idx < 2:
                                    a_prev = ("dve", E("dve", lambda v, pr=pr, half=half, bc=bc, p_p=p_p:
                                              v.tensor_scalar(kt[:, half * 2048:(half + 1) * 2048],
                                                              ps_qk[:], bc[:, pr:pr + 1], None,
                                                              ALU.add),
                                              waits=[("pe", p_p)]))
                                else:
                                    a_prev, d_tp = v_evac_transpose(
                                        pr, half, bc, p_p, d_tp, ps_qk, ps_tr)
                                    a_prev = ("dve", a_prev)

                            d_vdone = d_tp

                            ncol = [0]

                            def norm_rounds(a_ready):
                                for g in range(4):
                                    p_n = E("pe", lambda t, g=g: t.matmul(
                                        ps_nrm[:], ones_bf[:], exp1[:, g * 512:(g + 1) * 512],
                                        start=True, stop=True, skip_group_check=True),
                                        waits=[("act", a_ready), ("dve", b.n("dve"))])
                                    E("dve", lambda v, i=ncol[0], p_n=p_n: v.tensor_reduce(
                                        cmx[:, i:i + 1], ps_nrm[:], AX, ALU.max),
                                        waits=[("pe", p_n)])
                                    ncol[0] += 1

                            a_sq = E("act", lambda a: a.activation(exp1[:], qt[:], AF.Square),
                                     waits=[a_prev])
                            norm_rounds(a_sq)
                            for half in range(2):
                                a_sqk = E("act", lambda a, half=half: a.activation(
                                    exp1[:], kt[:, half * 2048:(half + 1) * 2048], AF.Square),
                                    waits=[("dve", b.n("dve"))])
                                norm_rounds(a_sqk)

                            def cb(v):
                                v.drain()
                                v.tensor_reduce(cmx[:, 12:13], cmx[:, 0:4], AX, ALU.max)
                                v.tensor_reduce(cmx[:, 13:14], cmx[:, 4:12], AX, ALU.max)
                                v.drain()
                                v.tensor_tensor(negc[:], cmx[:, 12:13], cmx[:, 13:14], ALU.add)
                                v.drain()
                                return v.tensor_scalar(negc[:], negc[:], -1.0 / 16.0,
                                                       lam12[:, 0:1], ALU.mult, ALU.subtract)
                            d_negc = E("dve", cb, waits=[("act", b.n("act"))])

                        for qg in range(2):
                            attn_qg(pr, qg, d_negc, d_vdone)

                g_attn = b.guard()
                if dbg and li == 1:
                    E("sy", lambda s: s.dma_start(dbg_ao, aoT[:]),
                      waits=b.gw(g_attn, skip=("sy",)), inck=16)
                    g_attn = b.guard()

                with nc.sbuf_tensor("wo_sl", [128, 1024], BF16) as wo_sl, \
                     nc.sbuf_tensor("t_ev", [128, 512], F32) as t_ev, \
                     nc.sbuf_tensor("wf_f", [128, 1024], F32) as wf_f, \
                     nc.sbuf_tensor("wsg", [128, 2 * 1024], BF16) as wsg, \
                     nc.psum_tensor("ps_o", [128, 512], F32) as ps_o:
                    prev_ev = [0]
                    prev_pe = [g_attn["pe"]]
                    for fc in range(8):
                        n_wo = E("gp", lambda g, fc=fc: g.dma_start(
                            wo_sl.rearrange("p (r c) -> p r c", r=R),
                            wo_d.rearrange("p (r o) -> p r o", r=R)[:, :, fc * 128:(fc + 1) * 128]),
                            waits=(b.gw(g_attn, skip=("gp",)) if fc == 0 else
                                   [("pe", prev_pe[0])]), inck=16)
                        d_sg = 0
                        if li == 0:
                            n_wf = E("sy", lambda s, fc=fc: s.dma_start(
                                wf_f.rearrange("p (r c) -> p r c", r=R),
                                wf_d.rearrange("p (r o) -> p r o", r=R)[:, :, fc * 128:(fc + 1) * 128]),
                                waits=([("pe", prev_pe[0])] if fc else
                                       b.gw(g_attn, skip=("sy",))), inck=16)
                            a_sg = E("act", lambda a: a.activation(wsg[:, 0:1024], wf_f[:],
                                                                   AF.Sign),
                                     waits=[("sy", n_wf)])

                            def sgsel(v):
                                v.tensor_scalar(wsg[:, 1024:2048], wsg[:, 0:1024],
                                                selc[:, 1:2], None, ALU.mult)
                                return v.tensor_scalar(wsg[:, 0:1024], wsg[:, 0:1024],
                                                       selc[:, 0:1], None, ALU.mult)
                            d_sg = E("dve", sgsel, waits=[("act", a_sg), ("dve", d_rep)])

                        for g in range(4):
                            def peO(t, fc=fc, g=g):
                                for r in range(R):
                                    ins = t.matmul(
                                        ps_o[:], wo_sl[:, r * 128:(r + 1) * 128],
                                        aoT[:, r * 2048 + g * 512:r * 2048 + (g + 1) * 512],
                                        start=(r == 0),
                                        stop=(li == 1 and r == R - 1),
                                        skip_group_check=True)
                                if li == 0:
                                    for wofs, src in ((0, xn_a), (1024, xn_b)):
                                        for r in range(R):
                                            ins = t.matmul(
                                                ps_o[:],
                                                wsg[:, wofs + r * 128:wofs + (r + 1) * 128],
                                                src[:, r * TOK + g * 512:r * TOK + (g + 1) * 512],
                                                start=False,
                                                stop=(wofs == 1024 and r == R - 1),
                                                skip_group_check=True)
                                return ins
                            wts = [("gp", n_wo), ("dve", prev_ev[0])]
                            if li == 0:
                                wts.append(("dve", d_sg))
                            p_o = E("pe", peO, waits=wts)
                            if li == 0:
                                d_ev0 = E("dve", lambda v, fc=fc: v.tensor_scalar(
                                    t_ev[:], ps_o[:], boc[:, fc:fc + 1], None, ALU.add),
                                    waits=[("pe", p_o)])
                                prev_ev[0] = E("dve", lambda v, fc=fc, g=g:
                                               v.tensor_tensor(
                                    x_sb[:, fc * TOK + g * 512:fc * TOK + (g + 1) * 512],
                                    x_sb[:, fc * TOK + g * 512:fc * TOK + (g + 1) * 512],
                                    t_ev[:], ALU.add),
                                    waits=[("dve", d_ev0), ("gp", g_x)])
                            else:
                                prev_ev[0] = E("dve", lambda v, fc=fc, g=g, p_o=p_o:
                                               v.tensor_tensor(
                                    x_sb[:, fc * TOK + g * 512:fc * TOK + (g + 1) * 512],
                                    x_sb[:, fc * TOK + g * 512:fc * TOK + (g + 1) * 512],
                                    ps_o[:], ALU.add),
                                    waits=[("pe", p_o), ("gp", g_x)])
                        prev_pe[0] = b.n("pe")

        if li == 1:
            g_p3 = b.guard()
            if dbg:
                E("sy", lambda s: s.dma_start(dbg_x3, x_sb[:]),
                  waits=b.gw(g_p3, skip=("sy",)), inck=16)
                g_p3 = b.guard()
            with nc.sbuf_tensor("xn_o", [128, R * TOK], BF16) as xn_o, \
                 nc.sbuf_tensor("weTb", [8, 2048], BF16) as weTb:
                n_xno = E("gp", lambda g: g.dma_start(xn_o[:], xn_o_d),
                          waits=b.gw(g_p3, skip=("gp",)), inck=16)
                if dbg:
                    E("sy", lambda s: s.dma_start(dbg_xn, xn_o[:]),
                      waits=[("gp", n_xno)], inck=16)

                with nc.sbuf_tensor("ltm", [128, 8], F32) as ltm, \
                     nc.sbuf_tensor("m8", [128, 8], F32) as m8, \
                     nc.sbuf_tensor("s12", [128, 2], F32) as s12, \
                     nc.sbuf_tensor("sgu", [128, 8], F32) as sgu, \
                     nc.sbuf_tensor("sel8", [128, 8], F32) as sel8, \
                     nc.sbuf_tensor("lt_sb", [8, 2048], F32) as lt_sb:

                    with nc.sbuf_tensor("rw1_sl", [128, 2 * 1024], BF16) as rw1_sl, \
                         nc.sbuf_tensor("rw2_sl", [128, 256], BF16) as rw2_sl, \
                         nc.sbuf_tensor("rb1_sb", [128, 32], F32) as rb1_sb, \
                         nc.sbuf_tensor("rb2_sb", [8, 1], F32) as rb2_sb, \
                         nc.sbuf_tensor("rlog", [128, 2 * 2048], BF16) as rlog, \
                         nc.psum_tensor("ps_r", [128, 2048], F32) as ps_r, \
                         nc.psum_tensor("ps_l", [8, 2048], F32) as ps_l:
                        E("sy", lambda s: s.dma_start(rb1_sb[:], rb1_d),
                          waits=b.gw(g_p3, skip=("sy",)), inck=16)
                        n_rb = E("sy", lambda s: s.dma_start(rb2_sb[:], rb2_d), inck=16)
                        n_rw2 = E("gp", lambda g: g.dma_start(rw2_sl[:], rw2_d), inck=16)
                        a_sl_prev = [0]
                        p_r2_prev = [0]
                        a_sl_hist = {}
                        for hc in range(32):
                            n_w1 = E("gp", lambda g, hc=hc: g.dma_start(
                                rw1_sl[:, (hc % 2) * 1024:(hc % 2 + 1) * 1024].rearrange(
                                    "p (r c) -> p r c", r=R),
                                rw1_d.rearrange("p (r h) -> p r h", r=R)[
                                    :, :, hc * 128:(hc + 1) * 128]),
                                waits=([("pe", p_r2_prev[0])] if hc >= 2 else
                                       [("gp", n_xno)]), inck=16)

                            def peR(t, hc=hc):
                                for g in range(4):
                                    for r in range(R):
                                        ins = t.matmul(
                                            ps_r[:, g * 512:(g + 1) * 512],
                                            rw1_sl[:, (hc % 2) * 1024 + r * 128:(hc % 2) * 1024 + (r + 1) * 128],
                                            xn_o[:, r * TOK + g * 512:r * TOK + (g + 1) * 512],
                                            start=(r == 0), stop=(r == R - 1),
                                            skip_group_check=True)
                                return ins
                            p_r = E("pe", peR,
                                    waits=[("gp", n_w1), ("gp", n_xno),
                                           ("act", a_sl_prev[0])])
                            a_sl = E("act", lambda a, hc=hc, p_r=p_r: a.activation(
                                rlog[:, (hc % 2) * 2048:(hc % 2 + 1) * 2048], ps_r[:], AF.Silu,
                                bias=rb1_sb[:, hc:hc + 1]),
                                waits=[("pe", p_r), ("pe", a_sl_hist.get(hc - 2, 0))])
                            a_sl_prev[0] = a_sl

                            def peR2(t, hc=hc):
                                for g in range(4):
                                    ins = t.matmul(
                                        ps_l[:, g * 512:(g + 1) * 512],
                                        rw2_sl[:, hc * 8:(hc + 1) * 8],
                                        rlog[:, (hc % 2) * 2048 + g * 512:(hc % 2) * 2048 + (g + 1) * 512],
                                        start=(hc == 0), stop=(hc == 31),
                                        skip_group_check=True)
                                return ins
                            p_r2 = E("pe", peR2, waits=[("act", a_sl), ("gp", n_rw2)])
                            p_r2_prev[0] = p_r2
                            a_sl_hist[hc] = p_r2

                        a_lt = E("dve", lambda v: v.tensor_scalar(
                            lt_sb[:], ps_l[:], rb2_sb[:, 0:1], None, ALU.add),
                            waits=[("pe", p_r2_prev[0]), ("sy", n_rb)])

                    with nc.psum_tensor("ps_t2", [128, 8], F32) as ps_t2, \
                         nc.psum_tensor("ps_t2b", [8, 128], F32) as ps_t2b:
                        d_t2_prev = [0]
                        for tt in range(16):
                            p_t = E("pe", lambda t, tt=tt: t.transpose(
                                ps_t2[:], lt_sb[:, tt * 128:(tt + 1) * 128], idn_f[0:8, 0:8]),
                                waits=[("dve", a_lt), ("dve", d_t2_prev[0])])

                            def t2(v):
                                v.tensor_copy(ltm[:], ps_t2[:])
                                v.drain()
                                v.max(out=m8[:], in_=ltm[:])
                                v.drain()
                                v.tensor_tensor(s12[:, 0:1], m8[:, 0:1], m8[:, 1:2], ALU.add)
                                v.drain()
                                return v.tensor_scalar(sgu[:], ltm[:], 2.0, s12[:, 0:1],
                                                       ALU.mult, ALU.subtract)
                            d_t2 = E("dve", t2, waits=[("pe", p_t)])
                            a_sg2 = E("act", lambda a: a.activation(sgu[:], sgu[:], AF.Sigmoid),
                                      waits=[("dve", d_t2)])

                            def t2b(v):
                                v.tensor_scalar(sel8[:], ltm[:], m8[:, 1:2], None, ALU.is_ge)
                                v.drain()
                                return v.tensor_tensor(sel8[:], sgu[:], sel8[:], ALU.mult)
                            d_t2c = E("dve", t2b, waits=[("act", a_sg2)])
                            p_t2b = E("pe", lambda t: t.transpose(ps_t2b[:], sel8[:], idn_f[:]),
                                      waits=[("dve", d_t2c)])
                            def wecp(v, tt=tt):
                                v.tensor_copy(weTb[:, tt * 128:(tt + 1) * 128], ps_t2b[:])
                                return v.drain()
                            d_t2_prev[0] = E("dve", wecp, waits=[("pe", p_t2b)])
                        d_weT = d_t2_prev[0]
                        if dbg:
                            E("sy", lambda s: s.dma_start(dbg_lt, lt_sb[:]),
                              waits=[("dve", a_lt)], inck=16)
                            E("gp", lambda g: g.dma_start(dbg_we, weTb[:]),
                              waits=[("dve", d_weT)], inck=16)

                    with nc.sbuf_tensor("b2_bf", [8, 1024], BF16) as b2_bf, \
                         nc.psum_tensor("ps_b2", [128, 512], F32) as ps_b2:
                        n_b2 = E("gp", lambda g: g.dma_start(b2_bf[:], b2_d), inck=16)
                        d_web = d_weT
                        prevb = [0]
                        for fc in range(8):
                            for g in range(4):
                                p_b = E("pe", lambda t, fc=fc, g=g: t.matmul(
                                    ps_b2[:], b2_bf[:, fc * 128:(fc + 1) * 128],
                                    weTb[:, g * 512:(g + 1) * 512], start=True, stop=True,
                                    skip_group_check=True),
                                    waits=[("gp", n_b2), ("dve", d_web), ("dve", prevb[0])])
                                prevb[0] = E("dve", lambda v, fc=fc, g=g, p_b=p_b:
                                             v.tensor_tensor(
                                    x_sb[:, fc * TOK + g * 512:fc * TOK + (g + 1) * 512],
                                    x_sb[:, fc * TOK + g * 512:fc * TOK + (g + 1) * 512],
                                    ps_b2[:], ALU.add), waits=[("pe", p_b), ("gp", g_x)])

                g_router = b.guard()

                with nc.sbuf_tensor("w1b", [128, R * 2048], BF16) as w1b, \
                     nc.sbuf_tensor("w2b", [128, 16 * 1024], BF16) as w2b, \
                     nc.sbuf_tensor("hT", [128, 16 * 1024], BF16) as hT, \
                     nc.sbuf_tensor("werep", [128, 2048], BF16) as werep, \
                     nc.sbuf_tensor("wesep", [1, 2048], BF16) as wesep, \
                     nc.sbuf_tensor("b1_sb", [128, 32], F32) as b1_sb, \
                     nc.sbuf_tensor("t_mo", [128, 512], BF16) as t_mo, \
                     nc.psum_tensor("ps_h0", [128, 1024], F32) as ps_h0, \
                     nc.psum_tensor("ps_h1", [128, 1024], F32) as ps_h1, \
                     nc.psum_tensor("ps_o0", [128, 512], F32) as ps_o0, \
                     nc.psum_tensor("ps_o1", [128, 512], F32) as ps_o1, \
                     nc.psum_tensor("ps_w", [128, 512], F32) as ps_w:
                    ps_h = (ps_h0, ps_h1)
                    ps_o2 = (ps_o0, ps_o1)
                    g_exp = b.guard()
                    p_w1_free = [0]
                    p_w2_free = [0]
                    a_h_cnt = {0: 0, 1: 0}
                    d_ev_cnt = [0, 0]
                    p_w2_done = [0]
                    for e in range(NE):
                        n_ws = E("sy", lambda s, e=e: s.dma_start(wesep[:], weTb[e:e + 1, :]),
                                 waits=(b.gw(g_router, skip=("sy",)) if e == 0 else
                                        [("pe", b.n("pe"))]), inck=16)
                        n_b1 = E("sy", lambda s, e=e: s.dma_start(b1_sb[:], b1_d[e]),
                                 waits=[("act", b.n("act"))], inck=16)
                        d_wr = 0
                        for g in range(4):
                            p_wr = E("pe", lambda t, g=g: t.matmul(
                                ps_w[:], ones_bf[0:1, :], wesep[0:1, g * 512:(g + 1) * 512],
                                start=True, stop=True, skip_group_check=True),
                                waits=[("sy", n_ws)] + ([("dve", d_wr)] if g else
                                                        [("dve", b.n("dve"))]))
                            d_wr = E("dve", lambda v, g=g, p_wr=p_wr: v.tensor_copy(
                                werep[:, g * 512:(g + 1) * 512], ps_w[:]),
                                waits=[("pe", p_wr)])

                        for H in range(2):
                            n_w1l = E("gp", lambda g, e=e, H=H: g.dma_start(w1b[:], w1_d[e, H]),
                                      waits=([("pe", p_w1_free[0])] if p_w1_free[0] else
                                             b.gw(g_exp, skip=("gp",))), inck=16)
                            n_w2l = E("gp", lambda g, e=e, H=H: g.dma_start(w2b[:], w2_d[e, H]),
                                      waits=([("pe", p_w2_free[0])] if p_w2_free[0] else
                                             b.gw(g_exp, skip=("gp",))), inck=16)
                            for th in range(2):
                                for hc in range(16):
                                    par = hc % 2
                                    wts = [("gp", n_w1l), ("sy", n_b1)]
                                    if a_h_cnt[par]:
                                        wts.append(("act", a_h_cnt[par]))

                                    def peW1(t, hc=hc, th=th, par=par):
                                        for g in range(2):
                                            for r in range(R):
                                                ins = t.matmul(
                                                    ps_h[par][:, g * 512:(g + 1) * 512],
                                                    w1b[:, r * 2048 + hc * 128:r * 2048 + hc * 128 + 128],
                                                    xn_o[:, r * TOK + th * 1024 + g * 512:r * TOK + th * 1024 + (g + 1) * 512],
                                                    start=(r == 0), stop=(r == R - 1),
                                                    skip_group_check=True)
                                        return ins
                                    p_h = E("pe", peW1, waits=wts)
                                    a_h_cnt[par] = E("act", lambda a, hc=hc, par=par, H=H, p_h=p_h:
                                                     a.activation(
                                        hT[:, hc * 1024:(hc + 1) * 1024], ps_h[par][:], AF.Silu,
                                        bias=b1_sb[:, H * 16 + hc:H * 16 + hc + 1]),
                                        waits=[("pe", p_h), ("pe", p_w2_done[0])])
                                if th == 1:
                                    p_w1_free[0] = b.n("pe")
                                a_hdone = b.n("act")

                                for fc in range(8):
                                    for g in range(2):
                                        par2 = (fc * 2 + g) % 2

                                        def peW2(t, fc=fc, g=g, par2=par2):
                                            for hc in range(16):
                                                ins = t.matmul(
                                                    ps_o2[par2][:],
                                                    w2b[:, hc * 1024 + fc * 128:hc * 1024 + fc * 128 + 128],
                                                    hT[:, hc * 1024 + g * 512:hc * 1024 + (g + 1) * 512],
                                                    start=(hc == 0), stop=(hc == 15),
                                                    skip_group_check=True)
                                            return ins
                                        p_o2 = E("pe", peW2, waits=[
                                            ("act", a_hdone), ("gp", n_w2l),
                                            ("dve", d_ev_cnt[par2])])

                                        def evac(v, fc=fc, g=g, th=th, par2=par2):
                                            v.tensor_tensor(
                                                t_mo[:], ps_o2[par2][:],
                                                werep[:, th * 1024 + g * 512:th * 1024 + (g + 1) * 512],
                                                ALU.mult)
                                            return v.tensor_tensor(
                                                x_sb[:, fc * TOK + th * 1024 + g * 512:fc * TOK + th * 1024 + (g + 1) * 512],
                                                x_sb[:, fc * TOK + th * 1024 + g * 512:fc * TOK + th * 1024 + (g + 1) * 512],
                                                t_mo[:], ALU.add)
                                        d_ev_cnt[par2] = E("dve", evac,
                                                           waits=[("pe", p_o2)])
                                p_w2_done[0] = b.n("pe")
                            p_w2_free[0] = b.n("pe")

        gF = b.guard()
        n_out = E("sy", lambda s: s.dma_start(xo_d, x_sb[:]),
                  waits=b.gw(gF, skip=("sy",)), inck=16)
        b.ops["sy"].append(lambda s, v=n_out, sh=b.s["sy"].h: s.wait_ge(sh, v))

        with nc.Block() as blk:
            @blk.tensor
            def _(t):
                for fn in b.ops["pe"]:
                    fn(t)

            @blk.scalar
            def _(a):
                for fn in b.ops["act"]:
                    fn(a)

            @blk.vector
            def _(v):
                for fn in b.ops["dve"]:
                    fn(v)

            @blk.gpsimd
            def _(g):
                for fn in b.ops["gp"]:
                    fn(g)

            @blk.sync
            def _(s):
                for fn in b.ops["sy"]:
                    fn(s)
    return nc


_NC_CACHE = {}


def _get_nc(li):
    if li not in _NC_CACHE:
        _NC_CACHE[li] = build_layer(li)
    return _NC_CACHE[li]


def _stage_x(x_full):
    per_core = []
    for c in range(NCORE):
        bq, j = c // 2, c % 2
        xa = _featmaj(x_full[bq, 0:TOK])
        xb = _featmaj(x_full[bq, TOK:SEQ])
        per_core.append({"xt_a": xa, "xt_b": xb, "xt_o": xa if j == 0 else xb})
    return per_core


def stage_w0(inputs):
    return {
        "wq": _lhsT_slab(np.ascontiguousarray(np.asarray(inputs["L0_attn_Wq"], dtype=np.float32).T)),
        "wk": _lhsT_slab(np.ascontiguousarray(np.asarray(inputs["L0_attn_Wk"], dtype=np.float32).T)),
        "wv": _lhsT_slab(np.ascontiguousarray(np.asarray(inputs["L0_attn_Wv"], dtype=np.float32).T)),
        "wo": _lhsT_slab(np.ascontiguousarray(np.asarray(inputs["L0_attn_Wo"], dtype=np.float32).T)),
        "wf": _lhsT_slab(np.ascontiguousarray(np.asarray(inputs["L0_ffn_weight"], dtype=np.float32).T)),
        "bq": _bias_col(np.asarray(inputs["L0_attn_bq"], dtype=np.float32)),
        "bk": _bias_col(np.asarray(inputs["L0_attn_bk"], dtype=np.float32)),
        "bv": _bias_col(np.asarray(inputs["L0_attn_bv"], dtype=np.float32)),
        "bo": _bias_col(np.asarray(inputs["L0_attn_bo"], dtype=np.float32)),
        "lam": np.asarray(inputs["L0_attn_prior_lambda"], dtype=np.float32).reshape(1, 1),
    }


def stage_w1(inputs):
    w1a = np.asarray(inputs["L1_ffn_w1"], dtype=np.float32)
    w2a = np.asarray(inputs["L1_ffn_w2"], dtype=np.float32)
    return {
        "wq": _lhsT_slab(np.ascontiguousarray(np.asarray(inputs["L1_attn_Wq"], dtype=np.float32).T)),
        "wk": _lhsT_slab(np.ascontiguousarray(np.asarray(inputs["L1_attn_Wk"], dtype=np.float32).T)),
        "wv": _lhsT_slab(np.ascontiguousarray(np.asarray(inputs["L1_attn_Wv"], dtype=np.float32).T)),
        "wo": _lhsT_slab(np.ascontiguousarray(np.asarray(inputs["L1_attn_Wo"], dtype=np.float32).T)),
        "bq": _bias_col(np.asarray(inputs["L1_attn_bq"], dtype=np.float32)),
        "bk": _bias_col(np.asarray(inputs["L1_attn_bk"], dtype=np.float32)),
        "bv": _bias_col(np.asarray(inputs["L1_attn_bv"], dtype=np.float32)),
        "rw1": _lhsT_slab(np.asarray(inputs["L1_ffn_rw1"], dtype=np.float32)),
        "rb1": np.ascontiguousarray(np.asarray(inputs["L1_ffn_rb1"], dtype=np.float32).reshape(32, 128).T),
        "rw2": _lhsT_slab(np.asarray(inputs["L1_ffn_rw2"], dtype=np.float32)),
        "rb2": np.asarray(inputs["L1_ffn_rb2"], dtype=np.float32).reshape(8, 1),
        "w1": np.ascontiguousarray(w1a.reshape(8, 8, 128, 2, 2048).transpose(0, 3, 2, 1, 4).reshape(8, 2, 128, R * 2048)),
        "b1": np.ascontiguousarray(np.asarray(inputs["L1_ffn_b1"], dtype=np.float32).reshape(8, 32, 128).transpose(0, 2, 1)),
        "w2": np.ascontiguousarray(w2a.reshape(8, 2, 16, 128, 1024).transpose(0, 1, 3, 2, 4).reshape(8, 2, 128, 16 * 1024)),
        "b2": np.asarray(inputs["L1_ffn_b2"], dtype=np.float32),
    }


def sel_inputs(c):
    j = c % 2
    return {"sel": np.array([[1.0 - j, float(j), 1.0 - j, float(j)]], dtype=np.float32)}


def run_l0(x, w0):
    nc0 = _get_nc(0)
    stage = _stage_x(x)
    in_maps = []
    for c in range(NCORE):
        m = dict(stage[c])
        m.update(w0)
        m.update(sel_inputs(c))
        in_maps.append(m)
    r0 = run_bass_kernel_spmd(nc0, in_maps, core_ids=list(range(NCORE)))
    x1 = np.empty((4, SEQ, 1024), dtype=np.float32)
    for c in range(NCORE):
        bq, j = c // 2, c % 2
        x1[bq, j * TOK:(j + 1) * TOK] = _unfeatmaj(np.asarray(r0.results[c]["xo"]))
    return x1


def run_l1(x1, w1):
    nc1 = _get_nc(1)
    stage = _stage_x(x1)
    in_maps = []
    for c in range(NCORE):
        j = c % 2
        m = dict(stage[c])
        m.update(w1)
        m.update(sel_inputs(c))
        m["emask_a"] = np.concatenate([np.ones((63, 1), np.float32),
                                       np.array([[1.0 if j else 0.0]], np.float32)], axis=0)
        m["emask_b"] = np.array([[0.0 if j else 1.0]], dtype=np.float32)
        in_maps.append(m)
    r1 = run_bass_kernel_spmd(nc1, in_maps, core_ids=list(range(NCORE)))
    out = np.empty((4, SEQ, 1024), dtype=np.float32)
    for c in range(NCORE):
        bq, j = c // 2, c % 2
        out[bq, j * TOK:(j + 1) * TOK] = _unfeatmaj(np.asarray(r1.results[c]["xo"]))
    return out


def kernel(**inputs):
    x = np.asarray(inputs["L0_attn_x"], dtype=np.float32)
    x1 = run_l0(x, stage_w0(inputs))
    return run_l1(x1, stage_w1(inputs))
